# version 01
# baseline (speedup 1.0000x reference)
import math
import contextlib
import numpy as np
import ml_dtypes
import concourse.bass as bass
import concourse.mybir as mybir
from concourse.bass_utils import run_bass_kernel_spmd

F32 = mybir.dt.float32
BF16 = mybir.dt.bfloat16
AF = mybir.ActivationFunctionType
ALU = mybir.AluOpType
AX = mybir.AxisListType
BF = ml_dtypes.bfloat16

ENGS = ("tensor", "vector", "scalar", "sync", "gpsimd")


NLANES = 8


class _Ref:
    __slots__ = ("eng", "inc", "count", "is_dma", "lane")
    def __init__(self, eng, is_dma=False):
        self.eng = eng
        self.inc = False
        self.count = None
        self.is_dma = is_dma
        self.lane = None


class Prog:

    def __init__(self, nc):
        self.nc = nc
        self.items = []

    def op(self, eng, fn, waits=()):
        r = _Ref(eng)
        self.items.append(("op", eng, fn, list(waits), r))
        return r

    def dma(self, q, fn, waits=()):
        r = _Ref(q, is_dma=True)
        self.items.append(("dma", q, fn, list(waits), r))
        return r

    def final_dma_waits(self):
        use = {}
        lane_rot = {}
        for kind, eng, fn, waits, ref in self.items:
            if kind != "dma":
                continue
            i = lane_rot.get(eng, 0)
            lane_rot[eng] = (i + 1) % NLANES
            use[(eng, i)] = use.get((eng, i), 0) + 16
        return [(q, l, c) for (q, l), c in use.items()]

    def emit(self, block, esem, dpools):
        prev_op = None
        for kind, eng, fn, waits, ref in self.items:
            for w in waits:
                if isinstance(w, _Ref) and not w.is_dma:
                    w.inc = True
            if kind == "op":
                if prev_op is not None and not (prev_op.eng == eng == "tensor"):
                    prev_op.inc = True
                prev_op = ref
        ecnt = {e: 0 for e in ENGS}
        lane_rot = {}
        lane_use = {}
        for kind, eng, fn, waits, ref in self.items:
            if kind == "op":
                if ref.inc:
                    ecnt[eng] += 1
                    ref.count = ecnt[eng]
            else:
                i = lane_rot.get(eng, 0)
                lane_rot[eng] = (i + 1) % NLANES
                ref.lane = i
                lane_use[(eng, i)] = lane_use.get((eng, i), 0) + 16
                ref.count = lane_use[(eng, i)]
        streams = {e: [] for e in ENGS}
        prev_op = None
        for kind, eng, fn, waits, ref in self.items:
            wl = []
            if kind == "dma" and ref.count > 16:
                wl.append((dpools[eng][ref.lane], ref.count - 16))
            if kind == "op" and prev_op is not None and not (prev_op.eng == eng == "tensor"):
                wl.append((esem[prev_op.eng], prev_op.count))
            for w in waits:
                if isinstance(w, _Ref):
                    if w.is_dma:
                        wl.append((dpools[w.eng][w.lane], w.count))
                    else:
                        wl.append((esem[w.eng], w.count))
                else:
                    wl.append(w)
            if kind == "op":
                prev_op = ref

            def emitfn(e, fn=fn, wl=wl, kind=kind, ref=ref, eng=eng):
                for s, c in wl:
                    e.wait_ge(s, c)
                inst = fn(e)
                if kind == "dma":
                    assert inst is not None, "dma fn must return instruction"
                    inst.then_inc(dpools[eng][ref.lane], 16)
                elif ref.inc:
                    assert inst is not None, f"op on {eng} needs last instruction for inc"
                    inst.then_inc(esem[eng], 1)
                return inst

            streams[eng].append(emitfn)
        def run_stream(e, fns):
            for f in fns:
                f(e)
        block.tensor(lambda e: run_stream(e, streams["tensor"]))
        block.vector(lambda e: run_stream(e, streams["vector"]))
        block.scalar(lambda e: run_stream(e, streams["scalar"]))
        block.sync(lambda e: run_stream(e, streams["sync"]))
        block.gpsimd(lambda e: run_stream(e, streams["gpsimd"]))


def _hadamard(n):
    h = np.array([[1.0]], np.float32)
    while h.shape[0] < n:
        h = np.block([[h, h], [h, -h]])
    return h.astype(np.float32)


def fwht_butterfly_chunks(p, tiles, tmp_tile, nbits):
    n = len(tiles)
    for lvl in range(nbits):
        step = 1 << lvl
        for base in range(0, n, 2 * step):
            for off in range(step):
                i, j = base + off, base + off + step
                a, b = tiles[i], tiles[j]
                t = tmp_tile[0]
                p.op("vector", lambda e, t=t, a=a, b=b: e.tensor_tensor(t, a, b, op=ALU.subtract))
                p.op("vector", lambda e, a=a, b=b: e.tensor_tensor(a, a, b, op=ALU.add))
                tiles[j] = t
                tmp_tile[0] = b
    return tiles




class LNKit:

    def __init__(self, nc, st, name, psA, psB, ones_s, onescol, cc, Tg):
        sb = lambda n, shape, dt: st.enter_context(nc.sbuf_tensor(name + n, shape, dt))
        self.psA, self.psB = psA, psB
        self.ones_s, self.onescol, self.cc = ones_s, onescol, cc
        self.Tg = Tg
        self.mu = sb("mu", [1, Tg], F32)
        self.var = sb("var", [1, Tg], F32)
        self.rstd = sb("rstd", [1, Tg], F32)
        self.brow = sb("brow", [1, Tg], F32)
        self.a16 = sb("a16", [1, Tg], BF16)
        self.b16 = sb("b16", [1, Tg], BF16)
        self.ab_t = sb("abt", [128, Tg], F32)
        self.bb_t = sb("bbt", [128, Tg], F32)

    def emit(self, p, xsrc_f32, xdst_b16, xsq, KC, ld_waits=()):
        Tg = self.Tg
        NT = (Tg + 511) // 512
        psA, psB = self.psA, self.psB
        mu, var, rstd, brow = self.mu, self.var, self.rstd, self.brow
        a16, b16, ab_t, bb_t = self.a16, self.b16, self.ab_t, self.bb_t
        p.op("vector", lambda e: e.tensor_copy(
            xdst_b16.ap().rearrange("p c t -> p (c t)"),
            xsrc_f32.ap().rearrange("p c t -> p (c t)")), waits=list(ld_waits))
        p.op("scalar", lambda e: e.activation(
            xsq.ap().rearrange("p c t -> p (c t)"),
            xdst_b16.ap().rearrange("p c t -> p (c t)"), AF.Square))
        for nt in range(NT):
            tsl = slice(nt * 512, min((nt + 1) * 512, Tg))
            w = tsl.stop - tsl.start
            def mms(e, tsl=tsl, w=w):
                for i in range(KC):
                    inst = e.matmul(psA[0:1, 0:w], self.onescol[:, :], xdst_b16[:, i, tsl],
                                    start=(i == 0), stop=(i == KC - 1))
                for i in range(KC):
                    inst = e.matmul(psB[0:1, 0:w], self.onescol[:, :], xsq[:, i, tsl],
                                    start=(i == 0), stop=(i == KC - 1))
                return inst
            p.op("tensor", mms)
            p.op("vector", lambda e, tsl=tsl, w=w: e.tensor_scalar(
                mu[0:1, tsl], psA[0:1, 0:w], 1.0 / D, None, op0=ALU.mult))
            p.op("vector", lambda e, tsl=tsl, w=w: e.tensor_scalar(
                var[0:1, tsl], psB[0:1, 0:w], 1.0 / D, None, op0=ALU.mult))
        p.op("scalar", lambda e: e.activation(brow[0:1, :], mu[0:1, :], AF.Square))
        p.op("vector", lambda e: e.tensor_tensor(var[0:1, :], var[0:1, :], brow[0:1, :], op=ALU.subtract))
        p.op("scalar", lambda e: e.activation(rstd[0:1, :], var[0:1, :], AF.Ln, bias=self.cc[0:1, 0:1]))
        p.op("scalar", lambda e: e.activation(rstd[0:1, :], rstd[0:1, :], AF.Exp, scale=self.cc[0:1, 1:2]))
        p.op("vector", lambda e: e.tensor_tensor(brow[0:1, :], mu[0:1, :], rstd[0:1, :], op=ALU.mult))
        p.op("vector", lambda e: e.tensor_scalar(brow[0:1, :], brow[0:1, :], -1.0, None, op0=ALU.mult))
        p.op("vector", lambda e: e.tensor_copy(a16[0:1, :], rstd[0:1, :]))
        p.op("vector", lambda e: e.tensor_copy(b16[0:1, :], brow[0:1, :]))
        for nt in range(NT):
            tsl = slice(nt * 512, min((nt + 1) * 512, Tg))
            w = tsl.stop - tsl.start
            p.op("tensor", lambda e, tsl=tsl, w=w: e.matmul(psA[:, 0:w], self.ones_s[:, :], a16[0:1, tsl], start=True, stop=True))
            p.op("tensor", lambda e, tsl=tsl, w=w: e.matmul(psB[:, 0:w], self.ones_s[:, :], b16[0:1, tsl], start=True, stop=True))
            p.op("vector", lambda e, tsl=tsl, w=w: e.tensor_copy(ab_t[:, tsl], psA[:, 0:w]))
            p.op("vector", lambda e, tsl=tsl, w=w: e.tensor_copy(bb_t[:, tsl], psB[:, 0:w]))
        for i in range(KC):
            p.op("vector", lambda e, i=i: e.tensor_tensor(xdst_b16[:, i, :], xdst_b16[:, i, :], ab_t[:, :], op=ALU.mult))
            p.op("vector", lambda e, i=i: e.tensor_tensor(xdst_b16[:, i, :], xdst_b16[:, i, :], bb_t[:, :], op=ALU.add))


D = 2048
INTER = 8192
TC = 2048
NCHUNK = D // 128


def build_l0ffn(Tg=512, with_attn_in=True, debug=False, ngroups_lim=None):
    nc = bass.Bass("TRN2", target_bir_lowering=False)
    ngroups = TC // Tg
    if ngroups_lim:
        ngroups = ngroups_lim
    KC = NCHUNK
    IC = INTER // 128

    x_fm = nc.dram_tensor("x_fm", [D, TC], F32, kind="ExternalInput")
    y0_fm = nc.dram_tensor("y0_fm", [D, TC], F32, kind="ExternalInput")
    w_upT = nc.dram_tensor("w_upT", [D, INTER], F32, kind="ExternalInput")
    w_gateT = nc.dram_tensor("w_gateT", [D, INTER], F32, kind="ExternalInput")
    w_downT = nc.dram_tensor("w_downT", [INTER, D], F32, kind="ExternalInput")
    su_ug = nc.dram_tensor("su_ug", [128, KC], F32, kind="ExternalInput")
    su_dp = nc.dram_tensor("su_dp", [128, IC], F32, kind="ExternalInput")
    wsc_up = nc.dram_tensor("wsc_up", [128, IC], F32, kind="ExternalInput")
    wsc_gate = nc.dram_tensor("wsc_gate", [128, IC], F32, kind="ExternalInput")
    wsc_dp = nc.dram_tensor("wsc_dp", [128, KC], F32, kind="ExternalInput")
    h128 = nc.dram_tensor("h128", [128, 128], BF16, kind="ExternalInput")
    onesrow = nc.dram_tensor("onesrow", [1, 128], BF16, kind="ExternalInput")
    ccol = nc.dram_tensor("ccol", [1, 2], F32, kind="ExternalInput")
    x2_fm = nc.dram_tensor("x2_fm", [D, TC], F32, kind="ExternalOutput")
    if debug:
        dbg_xh = nc.dram_tensor("dbg_xh", [D, 512], F32, kind="ExternalOutput")
        dbg_xt = nc.dram_tensor("dbg_xt", [D, 512], F32, kind="ExternalOutput")
        dbg_g = nc.dram_tensor("dbg_g", [INTER, 512], F32, kind="ExternalOutput")
        dbg_g2 = nc.dram_tensor("dbg_g2", [INTER, 512], F32, kind="ExternalOutput")
        dbg_gb = nc.dram_tensor("dbg_gb", [INTER, 512], F32, kind="ExternalOutput")

    st = contextlib.ExitStack()
    with st:
        sb = lambda name, shape, dt=BF16: st.enter_context(nc.sbuf_tensor(name, shape, dt))
        x1g = sb("x1g", [128, KC, Tg], F32)
        xh = sb("xh", [128, KC, Tg], BF16)
        y0g = sb("y0g", [128, KC, Tg], BF16)
        g_sb = sb("g_sb", [128, IC, Tg], BF16)
        silu_t = sb("silu_t", [128, Tg], BF16)
        wslot = sb("wslot", [128, KC, 512], BF16)
        wslot2 = sb("wslot2", [128, KC, 512], BF16)
        su_ug_s = sb("su_ug_s", [128, KC], F32)
        su_dp_s = sb("su_dp_s", [128, IC], F32)
        scu = sb("scu", [128, IC], F32)
        scg = sb("scg", [128, IC], F32)
        scd = sb("scd", [128, KC], F32)
        h128_s = sb("h128_s", [128, 128], BF16)
        ones_s = sb("ones_s", [1, 128], BF16)
        onescol = sb("onescol", [128, 1], BF16)
        cc = sb("cc", [1, 2], F32)
        tmp_bt = sb("tmp_bt", [128, Tg], BF16)
        xsq = y0g
        dview = lambda t: t.ap().rearrange("p c t -> p (c t)").rearrange("p (a b) -> p a b", a=IC)

        ps = [st.enter_context(nc.psum_tensor(f"ps{i}", [128, 512], F32)) for i in range(6)]
        LN = None
        esem = {e: st.enter_context(nc.semaphore(f"es_{e}")) for e in ENGS}
        dpools = {e: [st.enter_context(nc.semaphore(f"ds_{e}{i}")) for i in range(NLANES)]
                  for e in ("sync", "gpsimd")}
        block = st.enter_context(nc.Block())

        p = Prog(nc)

        consts = []
        consts.append(p.dma("sync", lambda e: e.dma_start(h128_s[:, :], h128[:, :])))
        consts.append(p.dma("sync", lambda e: e.dma_start(ones_s[:, :], onesrow[:, :])))
        consts.append(p.dma("sync", lambda e: e.dma_start(
            onescol[:, :], bass.AP(onesrow, 0, [[0, 128], [1, 1]]))))
        consts.append(p.dma("sync", lambda e: e.dma_start(cc[:, :], ccol[:, :])))
        consts.append(p.dma("sync", lambda e: e.dma_start(su_ug_s[:, :], su_ug[:, :])))
        consts.append(p.dma("sync", lambda e: e.dma_start(su_dp_s[:, :], su_dp[:, :])))
        consts.append(p.dma("sync", lambda e: e.dma_start(scu[:, :], wsc_up[:, :])))
        consts.append(p.dma("sync", lambda e: e.dma_start(scg[:, :], wsc_gate[:, :])))
        consts.append(p.dma("sync", lambda e: e.dma_start(scd[:, :], wsc_dp[:, :])))
        s2048 = 1.0 / math.sqrt(2048.0)
        s8192 = 1.0 / math.sqrt(8192.0)
        p.op("vector", lambda e: e.tensor_scalar(scu[:, :], scu[:, :], s2048, None, op0=ALU.mult), waits=consts[6:7])
        p.op("vector", lambda e: e.tensor_scalar(scg[:, :], scg[:, :], s2048, None, op0=ALU.mult), waits=consts[7:8])
        p.op("vector", lambda e: e.tensor_scalar(scd[:, :], scd[:, :], s8192, None, op0=ALU.mult), waits=consts[8:9])
        p.op("sync", lambda e: e.nop(), waits=list(consts))

        NT = Tg // 512

        def dump(dst, tiles_list, n):
            bar = p.op("gpsimd", lambda e: e.nop())
            refs = []
            for i in range(n):
                refs.append(p.dma("gpsimd", lambda e, i=i: e.dma_start(
                    bass.AP(dst, i * 128 * 512, [[512, 128], [1, 512]]),
                    tiles_list[i]), waits=[bar]))
            p.op("gpsimd", lambda e: e.nop(), waits=[refs[-1]])

        slot_last_use = {}
        prev_group_done = []

        for g in range(ngroups):
            tg = slice(g * Tg, (g + 1) * Tg)
            dx = p.dma("sync", lambda e, tg=tg: e.dma_start(
                x1g[:, :, :], bass.AP(x_fm, tg.start, [[TC, 128], [128 * TC, KC], [1, Tg]])),
                waits=list(prev_group_done))
            dy = p.dma("gpsimd", lambda e, tg=tg: e.dma_start(
                y0g[:, :, :], bass.AP(y0_fm, tg.start, [[TC, 128], [128 * TC, KC], [1, Tg]])),
                waits=list(prev_group_done))
            if with_attn_in:
                p.op("vector", lambda e: e.tensor_tensor(
                    x1g.ap().rearrange("p c t -> p (c t)"),
                    x1g.ap().rearrange("p c t -> p (c t)"),
                    y0g.ap().rearrange("p c t -> p (c t)"), op=ALU.add), waits=[dx, dy])
                ln_waits = []
            else:
                ln_waits = [dx]
            if LN is None:
                LN = LNKit(nc, st, "ln", ps[0], ps[1], ones_s, onescol, cc, Tg)
            LN.emit(p, x1g, xh, xsq, KC, ln_waits)
            if debug and g == 0:
                dump(dbg_xh, [xh[:, i, 0:512] for i in range(KC)], KC)
            for i in range(KC):
                p.op("vector", lambda e, i=i: e.tensor_scalar(
                    xh[:, i, :], xh[:, i, :], su_ug_s[:, i:i + 1], None, op0=ALU.mult))
            tiles = [xh[:, i, :] for i in range(KC)]
            tmp_holder = [tmp_bt[:, :]]
            tiles = fwht_butterfly_chunks(p, tiles, tmp_holder, 4)
            for i in range(KC):
                for nt in range(NT):
                    tsl = slice(nt * 512, (nt + 1) * 512)
                    pb = ps[i % 2]
                    p.op("tensor", lambda e, i=i, tsl=tsl, pb=pb: e.matmul(
                        pb[:, :], h128_s[:, :], tiles[i][:, tsl], start=True, stop=True))
                    p.op("vector", lambda e, i=i, tsl=tsl, pb=pb: e.tensor_copy(
                        tiles[i][:, tsl], pb[:, :]))
            xh_tiles = tiles
            if debug and g == 0:
                dump(dbg_xt, [xh_tiles[i][:, 0:512] for i in range(KC)], KC)

            for phase, (wmat, sccol) in enumerate(((w_upT, scu), (w_gateT, scg))):
                for blk in range(INTER // 512):
                    slot = wslot if blk % 2 == 0 else wslot2
                    war = [slot_last_use[id(slot)]] if id(slot) in slot_last_use else []
                    dw = p.dma("gpsimd", lambda e, wmat=wmat, blk=blk, slot=slot: e.dma_start(
                        slot[:, :, :],
                        bass.AP(wmat, blk * 512, [[INTER, 128], [128 * INTER, KC], [1, 512]])),
                        waits=war)
                    for sub in range(4):
                        ic = blk * 4 + sub
                        for nt in range(NT):
                            tsl = slice(nt * 512, (nt + 1) * 512)
                            pb = ps[2 + (ic % 2)]
                            def mmup(e, slot=slot, sub=sub, tsl=tsl, pb=pb):
                                for k in range(KC):
                                    inst = e.matmul(pb[:, :], slot[:, k, sub * 128:(sub + 1) * 128],
                                                    xh_tiles[k][:, tsl],
                                                    start=(k == 0), stop=(k == KC - 1))
                                return inst
                            mref = p.op("tensor", mmup, waits=[dw])
                            slot_last_use[id(slot)] = mref
                            if phase == 0:
                                p.op("scalar", lambda e, ic=ic, tsl=tsl, pb=pb, sccol=sccol: e.activation(
                                    g_sb[:, ic, tsl], pb[:, :], AF.Copy, scale=sccol[:, ic:ic + 1]))
                            else:
                                p.op("scalar", lambda e, ic=ic, tsl=tsl, pb=pb, sccol=sccol: e.activation(
                                    silu_t[:, tsl], pb[:, :], AF.Silu, scale=sccol[:, ic:ic + 1]))
                                p.op("vector", lambda e, ic=ic, tsl=tsl: e.tensor_tensor(
                                    g_sb[:, ic, tsl], g_sb[:, ic, tsl], silu_t[:, tsl], op=ALU.mult))

            if debug and g == 0:
                dump(dbg_g, [g_sb[:, ic, 0:512] for ic in range(IC)], IC)
            for ic in range(IC):
                p.op("vector", lambda e, ic=ic: e.tensor_scalar(
                    g_sb[:, ic, :], g_sb[:, ic, :], su_dp_s[:, ic:ic + 1], None, op0=ALU.mult))
            gtiles = [g_sb[:, ic, :] for ic in range(IC)]
            tmp_holder = [tmp_bt[:, :]]
            gtiles = fwht_butterfly_chunks(p, gtiles, tmp_holder, 6)
            if debug and g == 0:
                dump(dbg_gb, [gtiles[ic][:, 0:512] for ic in range(IC)], IC)
            for ic in range(IC):
                for nt in range(NT):
                    tsl = slice(nt * 512, (nt + 1) * 512)
                    pb = ps[ic % 2]
                    p.op("tensor", lambda e, ic=ic, tsl=tsl, pb=pb: e.matmul(
                        pb[:, :], h128_s[:, :], gtiles[ic][:, tsl], start=True, stop=True))
                    p.op("vector", lambda e, ic=ic, tsl=tsl, pb=pb: e.tensor_copy(
                        gtiles[ic][:, tsl], pb[:, :]))

            if debug and g == 0:
                dump(dbg_g2, [gtiles[ic][:, 0:512] for ic in range(IC)], IC)
            prev_group_done = []
            for blk in range(KC):
                slot = wslot if blk % 2 == 0 else wslot2
                war = [slot_last_use[id(slot)]] if id(slot) in slot_last_use else []
                dw = p.dma("gpsimd", lambda e, blk=blk, slot=slot: e.dma_start(
                    dview(slot),
                    bass.AP(w_downT, blk * 128, [[D, 128], [128 * D, IC], [1, 128]])),
                    waits=war)
                for nt in range(NT):
                    tsl = slice(nt * 512, (nt + 1) * 512)
                    pb = ps[4 + (nt % 2)]
                    def mmdn(e, slot=slot, tsl=tsl, pb=pb):
                        sv = dview(slot)
                        for k in range(IC):
                            inst = e.matmul(pb[:, :], sv[:, k, :], gtiles[k][:, tsl],
                                            start=(k == 0), stop=(k == IC - 1))
                        return inst
                    mref = p.op("tensor", mmdn, waits=[dw])
                    slot_last_use[id(slot)] = mref
                    r = p.op("vector", lambda e, blk=blk, tsl=tsl, pb=pb: e.scalar_tensor_tensor(
                        x1g[:, blk, tsl], pb[:, :], scd[:, blk:blk + 1], x1g[:, blk, tsl],
                        op0=ALU.mult, op1=ALU.add))
                ds = p.dma("sync", lambda e, blk=blk, tg=tg: e.dma_start(
                    bass.AP(x2_fm, blk * 128 * TC + tg.start, [[TC, 128], [1, Tg]]),
                    x1g[:, blk, :]), waits=[r])
                prev_group_done.append(ds)
            prev_group_done = prev_group_done[-1:]

        fin = [(dpools[q][l], c) for q, l, c in p.final_dma_waits()]
        p.op("sync", lambda e: e.nop(), waits=fin)

        p.emit(block, esem, dpools)
    return nc


def host_l0ffn_inputs(inputs, xcur, y0):
    H = _hadamard(128)
    maps = []
    for c in range(8):
        b, q = divmod(c, 4)
        tsl = slice(q * TC, (q + 1) * TC)
        m = {
            "x_fm": np.ascontiguousarray(xcur[b, tsl].T),
            "y0_fm": np.ascontiguousarray(y0[b, tsl].T.astype(np.float32)),
            "w_upT": np.ascontiguousarray(inputs["L0_ffn_W_up"].T),
            "w_gateT": np.ascontiguousarray(inputs["L0_ffn_W_gate"].T),
            "w_downT": np.ascontiguousarray(inputs["L0_ffn_W_down"].T),
            "su_ug": np.ascontiguousarray(inputs["L0_ffn_SU_ug"].reshape(16, 128).T),
            "su_dp": np.ascontiguousarray(inputs["L0_ffn_SU_dp"].reshape(64, 128).T),
            "wsc_up": np.ascontiguousarray(inputs["L0_ffn_Wscale_ug"][:INTER].reshape(64, 128).T),
            "wsc_gate": np.ascontiguousarray(inputs["L0_ffn_Wscale_ug"][INTER:].reshape(64, 128).T),
            "wsc_dp": np.ascontiguousarray(inputs["L0_ffn_Wscale_dp"].reshape(16, 128).T),
            "h128": H.astype(BF),
            "onesrow": np.ones((1, 128), BF),
            "ccol": np.array([[1e-5, -0.5]], np.float32),
        }
        maps.append(m)
    return maps





TB = 8192
NHC = 4


def build_l0attn(Tg=512, ngroups_lim=None):
    nc = bass.Bass("TRN2", target_bir_lowering=False)
    ngroups = TB // Tg if not ngroups_lim else ngroups_lim
    KC = NCHUNK
    NCH = Tg // 128

    x_fm = nc.dram_tensor("x_fm", [D, TB], F32, kind="ExternalInput")
    wq_s = nc.dram_tensor("wq_s", [D, 512], F32, kind="ExternalInput")
    wk_s = nc.dram_tensor("wk_s", [D, 512], F32, kind="ExternalInput")
    wv_s = nc.dram_tensor("wv_s", [D, 512], F32, kind="ExternalInput")
    waba = nc.dram_tensor("waba", [D, 4], F32, kind="ExternalInput")
    wabb = nc.dram_tensor("wabb", [D, 4], F32, kind="ExternalInput")
    bq_c = nc.dram_tensor("bq_c", [128, 4], F32, kind="ExternalInput")
    bk_r = nc.dram_tensor("bk_r", [1, 512], F32, kind="ExternalInput")
    bv_r = nc.dram_tensor("bv_r", [1, 512], F32, kind="ExternalInput")
    bal_c = nc.dram_tensor("bal_c", [4, 1], F32, kind="ExternalInput")
    bbe_c = nc.dram_tensor("bbe_c", [4, 1], F32, kind="ExternalInput")
    u_tri = nc.dram_tensor("u_tri", [128, 128], BF16, kind="ExternalInput")
    negu = nc.dram_tensor("negu", [128, 128], BF16, kind="ExternalInput")
    ones2d = nc.dram_tensor("ones2d", [128, 128], BF16, kind="ExternalInput")
    maskneg = nc.dram_tensor("maskneg", [128, 128], F32, kind="ExternalInput")
    ident = nc.dram_tensor("ident", [128, 128], BF16, kind="ExternalInput")
    onesrow = nc.dram_tensor("onesrow", [1, 128], BF16, kind="ExternalInput")
    ccol = nc.dram_tensor("ccol", [1, 2], F32, kind="ExternalInput")
    cc4 = nc.dram_tensor("cc4", [4, 3], F32, kind="ExternalInput")
    cdscr = nc.dram_tensor("cdscr", [4, 128], F32, kind="Internal")
    y_fm = nc.dram_tensor("y_fm", [512, TB], BF16, kind="ExternalOutput")

    st = contextlib.ExitStack()
    with st:
        sb = lambda name, shape, dt=BF16: st.enter_context(nc.sbuf_tensor(name, shape, dt))
        xg = sb("xg", [128, KC, Tg], F32)
        xh = sb("xh", [128, KC, Tg], BF16)
        xsq = sb("xsq", [128, KC, Tg], BF16)
        wq_sb = sb("wq_sb", [128, KC, 512], BF16)
        wk_sb = sb("wk_sb", [128, KC, 512], BF16)
        wv_sb = sb("wv_sb", [128, KC, 512], BF16)
        waba_sb = sb("waba_sb", [128, KC, 4], BF16)
        wabb_sb = sb("wabb_sb", [128, KC, 4], BF16)
        bq_sb = sb("bq_sb", [128, 4], F32)
        bkb = sb("bkb", [128, 512], F32)
        bvb = sb("bvb", [128, 512], F32)
        bal_sb = sb("bal_sb", [4, 1], F32)
        bbe_sb = sb("bbe_sb", [4, 1], F32)
        u_sb = sb("u_sb", [128, 128], BF16)
        negu_sb = sb("negu_sb", [128, 128], BF16)
        ones2_sb = sb("ones2_sb", [128, 128], BF16)
        mneg_sb = sb("mneg_sb", [128, 128], F32)
        id_sb = sb("id_sb", [128, 128], BF16)
        ones_s = sb("ones_s", [1, 128], BF16)
        onescol = sb("onescol", [128, 1], BF16)
        cc = sb("cc", [1, 2], F32)
        c4 = sb("c4", [4, 3], F32)
        q_fm = sb("q_fm", [128, NHC, Tg], BF16)
        k_tm = sb("k_tm", [128, NCH, 512], BF16)
        v_tm = sb("v_tm", [128, NCH, 512], BF16)
        k_raw = sb("k_raw", [128, 512], F32)
        ksq = sb("ksq", [128, 512], F32)
        knrm = sb("knrm", [128, 1], F32)
        y_out = sb("y_out", [128, NHC, Tg], BF16)
        stateT = sb("stateT", [128, NHC, 128], F32)
        stateT2 = sb("stateT2", [128, NHC, 128], F32)
        st16 = sb("st16", [128, NHC, 128], BF16)
        al_hm = sb("al_hm", [4, Tg], F32)
        u_hm = sb("u_hm", [4, Tg], F32)
        t1_hm = sb("t1_hm", [4, Tg], F32)
        be_hm = sb("be_hm", [4, Tg], F32)
        la16 = sb("la16", [4, Tg], BF16)
        be16 = sb("be16", [4, Tg], BF16)
        cla_hm = sb("cla_hm", [4, Tg], F32)
        zeros4 = sb("zeros4", [4, 128], F32)
        cd32 = sb("cd32", [4, 128], F32)
        d2e16 = sb("d2e16", [4, 128], BF16)
        w16 = sb("w16", [4, 128], BF16)
        CDb = sb("CDb", [128, NHC, 128], F32)
        la_tm = sb("la_tm", [128, 4], F32)
        be_tm = sb("be_tm", [128, 4], F32)
        w_tm = sb("w_tm", [128, 4], F32)
        lt1 = sb("lt1", [128, 128], BF16)
        lt2 = sb("lt2", [128, 128], BF16)
        dlog = sb("dlog", [128, 128], F32)
        d16 = sb("d16", [128, 128], BF16)
        p16 = sb("p16", [128, 128], BF16)
        kf16 = sb("kf16", [128, 128], BF16)
        qt16 = sb("qt16", [128, 128], BF16)
        kw16 = sb("kw16", [128, 128], BF16)

        ps = [st.enter_context(nc.psum_tensor(f"ps{i}", [128, 512], F32)) for i in range(6)]
        psb = [st.enter_context(nc.psum_tensor(f"psb{i}", [128, 512], BF16)) for i in range(2)]
        esem = {e: st.enter_context(nc.semaphore(f"es_{e}")) for e in ENGS}
        dpools = {e: [st.enter_context(nc.semaphore(f"ds_{e}{i}")) for i in range(NLANES)]
                  for e in ("sync", "gpsimd")}
        block = st.enter_context(nc.Block())
        p = Prog(nc)

        cst = []
        for dst, srcd in ((u_sb, u_tri), (negu_sb, negu), (ones2_sb, ones2d),
                          (mneg_sb, maskneg), (id_sb, ident), (ones_s, onesrow),
                          (cc, ccol), (c4, cc4), (bq_sb, bq_c), (bal_sb, bal_c),
                          (bbe_sb, bbe_c)):
            cst.append(p.dma("sync", lambda e, dst=dst, srcd=srcd: e.dma_start(dst[:, :], srcd[:, :])))
        cst.append(p.dma("sync", lambda e: e.dma_start(
            onescol[:, :], bass.AP(onesrow, 0, [[0, 128], [1, 1]]))))
        for dst, srcd, w in ((wq_sb, wq_s, 512), (wk_sb, wk_s, 512), (wv_sb, wv_s, 512),
                             (waba_sb, waba, 4), (wabb_sb, wabb, 4)):
            cst.append(p.dma("gpsimd", lambda e, dst=dst, srcd=srcd, w=w: e.dma_start(
                dst[:, :, :], bass.AP(srcd, 0, [[w, 128], [128 * w, KC], [1, w]]))))
        cst.append(p.dma("gpsimd", lambda e: e.dma_start(
            bkb[:, :], bass.AP(bk_r, 0, [[0, 128], [1, 512]]))))
        cst.append(p.dma("gpsimd", lambda e: e.dma_start(
            bvb[:, :], bass.AP(bv_r, 0, [[0, 128], [1, 512]]))))
        cready = p.op("vector", lambda e: e.memset(zeros4[:, :], 0.0), waits=cst)
        p.op("vector", lambda e: e.memset(stateT.ap().rearrange("p h d -> p (h d)"), 0.0))
        p.op("vector", lambda e: e.memset(stateT2.ap().rearrange("p h d -> p (h d)"), 0.0))
        p.op("vector", lambda e: e.memset(st16.ap().rearrange("p h d -> p (h d)"), 0.0))

        LN = LNKit(nc, st, "ln", ps[0], ps[1], ones_s, onescol, cc, Tg)
        prev_group_done = []

        for g in range(ngroups):
            tg = slice(g * Tg, (g + 1) * Tg)
            dx = p.dma("sync", lambda e, tg=tg: e.dma_start(
                xg[:, :, :], bass.AP(x_fm, tg.start, [[TB, 128], [128 * TB, KC], [1, Tg]])),
                waits=list(prev_group_done))
            LN.emit(p, xg, xh, xsq, KC, [dx])

            for h in range(NHC):
                def mmq(e, h=h):
                    for k in range(KC):
                        inst = e.matmul(ps[2][:, :], wq_sb[:, k, 128 * h:128 * (h + 1)],
                                        xh[:, k, :], start=(k == 0), stop=(k == KC - 1))
                    return inst
                p.op("tensor", mmq)
                p.op("vector", lambda e, h=h: e.tensor_scalar(
                    q_fm[:, h, :], ps[2][:, :], bq_sb[:, h:h + 1], None, op0=ALU.add))
            for ch in range(NCH):
                csl = slice(ch * 128, (ch + 1) * 128)
                def mmk(e, csl=csl):
                    for k in range(KC):
                        inst = e.matmul(ps[2][:, :], xh[:, k, csl], wk_sb[:, k, :],
                                        start=(k == 0), stop=(k == KC - 1))
                    return inst
                p.op("tensor", mmk)
                p.op("vector", lambda e: e.tensor_tensor(k_raw[:, :], ps[2][:, :], bkb[:, :], op=ALU.add))
                p.op("scalar", lambda e: e.activation(ksq[:, :], k_raw[:, :], AF.Square))
                for h in range(NHC):
                    hsl = slice(h * 128, (h + 1) * 128)
                    p.op("vector", lambda e, hsl=hsl: e.reduce_sum(knrm[:, 0:1], ksq[:, hsl], axis=AX.X))
                    p.op("scalar", lambda e: e.activation(knrm[:, 0:1], knrm[:, 0:1], AF.Sqrt))
                    p.op("vector", lambda e: e.tensor_scalar(knrm[:, 0:1], knrm[:, 0:1], 1e-12, None, op0=ALU.max))
                    p.op("vector", lambda e: e.reciprocal(knrm[:, 0:1], knrm[:, 0:1]))
                    p.op("vector", lambda e, ch=ch, hsl=hsl: e.tensor_scalar(
                        k_tm[:, ch, hsl], k_raw[:, hsl], knrm[:, 0:1], None, op0=ALU.mult))
                def mmv(e, csl=csl):
                    for k in range(KC):
                        inst = e.matmul(ps[2][:, :], xh[:, k, csl], wv_sb[:, k, :],
                                        start=(k == 0), stop=(k == KC - 1))
                    return inst
                p.op("tensor", mmv)
                p.op("vector", lambda e, ch=ch: e.tensor_tensor(v_tm[:, ch, :], ps[2][:, :], bvb[:, :], op=ALU.add))

            def mma(e):
                for k in range(KC):
                    inst = e.matmul(ps[2][0:4, 0:Tg], waba_sb[:, k, :], xh[:, k, :],
                                    start=(k == 0), stop=(k == KC - 1))
                return inst
            p.op("tensor", mma)
            p.op("scalar", lambda e: e.activation(al_hm[:, :], ps[2][0:4, 0:Tg], AF.Sigmoid, bias=bal_sb[:, 0:1]))
            p.op("scalar", lambda e: e.activation(la16[:, :], al_hm[:, :], AF.Ln, bias=c4[:, 0:1]))
            def mmb(e):
                for k in range(KC):
                    inst = e.matmul(ps[2][0:4, 0:Tg], wabb_sb[:, k, :], xh[:, k, :],
                                    start=(k == 0), stop=(k == KC - 1))
                return inst
            p.op("tensor", mmb)
            p.op("vector", lambda e: e.tensor_scalar(u_hm[:, :], ps[2][0:4, 0:Tg], bbe_sb[:, 0:1], None, op0=ALU.add))
            p.op("scalar", lambda e: e.activation(t1_hm[:, :], u_hm[:, :], AF.Abs))
            p.op("scalar", lambda e: e.activation(t1_hm[:, :], t1_hm[:, :], AF.Exp, scale=c4[:, 2:3]))
            p.op("scalar", lambda e: e.activation(t1_hm[:, :], t1_hm[:, :], AF.Ln, bias=c4[:, 1:2]))
            p.op("vector", lambda e: e.tensor_scalar(be_hm[:, :], u_hm[:, :], 0.0, None, op0=ALU.max))
            p.op("vector", lambda e: e.tensor_tensor(be_hm[:, :], be_hm[:, :], t1_hm[:, :], op=ALU.add))
            p.op("vector", lambda e: e.tensor_copy(be16[:, :], be_hm[:, :]))
            for ch in range(NCH):
                csl = slice(ch * 128, (ch + 1) * 128)
                p.op("vector", lambda e, csl=csl: e.tensor_tensor_scan(
                    cla_hm[:, csl], la16[:, csl], zeros4[:, :], 0.0, op0=ALU.add, op1=ALU.add))

            for ch in range(NCH):
                csl = slice(ch * 128, (ch + 1) * 128)
                gchunk = g * NCH + ch
                wcd = p.op("scalar", lambda e, csl=csl: e.activation(cd32[:, :], cla_hm[:, csl], AF.Exp))
                dcd1 = p.dma("sync", lambda e: e.dma_start(cdscr[:, :], cd32[:, :]), waits=[wcd])
                dcd2 = p.dma("sync", lambda e: e.dma_start(
                    CDb.ap().rearrange("p h c -> p (h c)"),
                    bass.AP(cdscr, 0, [[0, 128], [1, 512]])), waits=[dcd1])
                p.op("vector", lambda e, csl=csl: e.tensor_scalar(
                    t1_hm[:, 0:128], cla_hm[:, csl], cla_hm[:, csl.stop - 1:csl.stop], None, op0=ALU.subtract))
                p.op("scalar", lambda e: e.activation(d2e16[:, :], t1_hm[:, 0:128], AF.Exp, scale=c4[:, 2:3]))
                p.op("vector", lambda e, csl=csl: e.tensor_tensor(w16[:, :], be16[:, csl], d2e16[:, :], op=ALU.mult))
                p.op("tensor", lambda e, csl=csl: e.transpose(psb[0][0:128, 0:4], la16[0:4, csl], id_sb[0:4, 0:4]))
                p.op("vector", lambda e: e.tensor_copy(la_tm[:, :], psb[0][0:128, 0:4]))
                p.op("tensor", lambda e, csl=csl: e.transpose(psb[0][0:128, 0:4], be16[0:4, csl], id_sb[0:4, 0:4]))
                p.op("vector", lambda e: e.tensor_copy(be_tm[:, :], psb[0][0:128, 0:4]))
                p.op("tensor", lambda e: e.transpose(psb[0][0:128, 0:4], w16[0:4, 0:128], id_sb[0:4, 0:4]))
                p.op("vector", lambda e: e.tensor_copy(w_tm[:, :], psb[0][0:128, 0:4]))

                for h in range(NHC):
                    hsl = slice(h * 128, (h + 1) * 128)
                    p.op("vector", lambda e, h=h: e.tensor_scalar(
                        lt1[:, :], ones2_sb[:, :], la_tm[:, h:h + 1], None, op0=ALU.mult))
                    p.op("vector", lambda e, h=h: e.tensor_scalar(
                        lt2[:, :], negu_sb[:, :], la_tm[:, h:h + 1], None, op0=ALU.mult))
                    def mmr(e):
                        e.matmul(ps[3][:, 0:128], lt1[:, :], u_sb[:, :], start=True, stop=False)
                        return e.matmul(ps[3][:, 0:128], lt2[:, :], ones2_sb[:, :], start=False, stop=True)
                    p.op("tensor", mmr, waits=[dcd2])
                    p.op("vector", lambda e: e.tensor_tensor(dlog[:, :], ps[3][:, 0:128], mneg_sb[:, :], op=ALU.add))
                    p.op("scalar", lambda e: e.activation(d16[:, :], dlog[:, :], AF.Exp))
                    p.op("tensor", lambda e, ch=ch, hsl=hsl: e.transpose(
                        psb[1][0:128, 0:128], k_tm[:, ch, hsl], id_sb[:, :]))
                    p.op("vector", lambda e: e.tensor_copy(kf16[:, :], psb[1][0:128, 0:128]))
                    p.op("tensor", lambda e, h=h, csl=csl: e.matmul(
                        ps[4][:, 0:128], kf16[:, :], q_fm[:, h, csl], start=True, stop=True))
                    p.op("vector", lambda e: e.tensor_tensor(p16[:, :], ps[4][:, 0:128], d16[:, :], op=ALU.mult))
                    p.op("vector", lambda e, h=h: e.tensor_scalar(
                        p16[:, :], p16[:, :], be_tm[:, h:h + 1], None, op0=ALU.mult))
                    p.op("vector", lambda e, h=h, csl=csl: e.tensor_tensor(
                        qt16[:, :], q_fm[:, h, csl], CDb[:, h, :], op=ALU.mult))
                    def mmy(e, h=h, ch=ch, hsl=hsl):
                        e.matmul(ps[5][:, 0:128], st16[:, h, :], qt16[:, :], start=True, stop=False)
                        return e.matmul(ps[5][:, 0:128], v_tm[:, ch, hsl], p16[:, :], start=False, stop=True)
                    p.op("tensor", mmy)
                    p.op("vector", lambda e, h=h, csl=csl: e.tensor_copy(
                        y_out[:, h, csl], ps[5][:, 0:128]))
                    p.op("vector", lambda e, ch=ch, hsl=hsl, h=h: e.tensor_scalar(
                        kw16[:, :], k_tm[:, ch, hsl], w_tm[:, h:h + 1], None, op0=ALU.mult))
                    p.op("tensor", lambda e, ch=ch, hsl=hsl: e.matmul(
                        ps[3][:, 0:128], kw16[:, :], v_tm[:, ch, hsl], start=True, stop=True))
                    s_old = stateT if (gchunk % 2 == 0) else stateT2
                    s_new = stateT2 if (gchunk % 2 == 0) else stateT
                    p.op("vector", lambda e, h=h, s_old=s_old, s_new=s_new: e.scalar_tensor_tensor(
                        s_new[:, h, :], s_old[:, h, :], CDb[:, h, 127:128], ps[3][:, 0:128],
                        op0=ALU.mult, op1=ALU.add))
                    p.op("vector", lambda e, h=h, s_new=s_new: e.tensor_copy(st16[:, h, :], s_new[:, h, :]))

            prev_group_done = []
            r = p.op("vector", lambda e: e.tensor_copy(y_out[:, 0, 0:1], y_out[:, 0, 0:1]))
            dsy = p.dma("sync", lambda e, tg=tg: e.dma_start(
                bass.AP(y_fm, tg.start, [[TB, 128], [128 * TB, NHC], [1, Tg]]),
                y_out.ap()), waits=[r])
            prev_group_done.append(dsy)

        fin = [(dpools[q][l], c) for q, l, c in p.final_dma_waits()]
        p.op("sync", lambda e: e.nop(), waits=fin)
        p.emit(block, esem, dpools)
    return nc


def host_l0attn_inputs(inputs):
    H = None
    U = np.triu(np.ones((128, 128), np.float32)).astype(BF)
    maps = []
    x = inputs["L0_attn_x"].astype(np.float32)
    for c in range(8):
        b = c // 4
        hs = slice(4 * (c % 4) * 128, (4 * (c % 4) + 4) * 128)
        hh = slice(4 * (c % 4), 4 * (c % 4) + 4)
        m = {
            "x_fm": np.ascontiguousarray(x[b].T),
            "wq_s": np.ascontiguousarray(inputs["L0_attn_Wq"][:, hs]),
            "wk_s": np.ascontiguousarray(inputs["L0_attn_Wk"][:, hs]),
            "wv_s": np.ascontiguousarray(inputs["L0_attn_Wv"][:, hs]),
            "waba": np.ascontiguousarray(inputs["L0_attn_Walpha"][:, hh]),
            "wabb": np.ascontiguousarray(inputs["L0_attn_Wbeta"][:, hh]),
            "bq_c": np.ascontiguousarray(inputs["L0_attn_bq"][hs].reshape(4, 128).T),
            "bk_r": np.ascontiguousarray(inputs["L0_attn_bk"][hs].reshape(1, 512)),
            "bv_r": np.ascontiguousarray(inputs["L0_attn_bv"][hs].reshape(1, 512)),
            "bal_c": np.ascontiguousarray(inputs["L0_attn_balpha"][hh].reshape(4, 1)),
            "bbe_c": np.ascontiguousarray(inputs["L0_attn_bbeta"][hh].reshape(4, 1)),
            "u_tri": U,
            "negu": (-U.astype(np.float32)).astype(BF),
            "ones2d": np.ones((128, 128), BF),
            "maskneg": np.where(np.arange(128)[:, None] <= np.arange(128)[None, :], 0.0, -30000.0).astype(np.float32),
            "ident": np.eye(128, dtype=np.float32).astype(BF),
            "onesrow": np.ones((1, 128), BF),
            "ccol": np.array([[1e-5, -0.5]], np.float32),
            "cc4": np.tile(np.array([[1e-6, 1.0, -1.0]], np.float32), (4, 1)),
        }
        maps.append(m)
    return maps


def build_mamba_pre(Tg=512, ngroups_lim=None):
    nc = bass.Bass("TRN2", target_bir_lowering=False)
    ngroups = TC // Tg if not ngroups_lim else ngroups_lim
    KC = NCHUNK
    x_fm = nc.dram_tensor("x_fm", [D, TC], F32, kind="ExternalInput")
    w_xT = nc.dram_tensor("w_xT", [D, 96], F32, kind="ExternalInput")
    onesrow = nc.dram_tensor("onesrow", [1, 128], BF16, kind="ExternalInput")
    ccol = nc.dram_tensor("ccol", [1, 2], F32, kind="ExternalInput")
    xh_out = nc.dram_tensor("xh_out", [D, TC], BF16, kind="ExternalOutput")
    xp_out = nc.dram_tensor("xp_out", [96, TC], F32, kind="ExternalOutput")

    st = contextlib.ExitStack()
    with st:
        sb = lambda name, shape, dt=BF16: st.enter_context(nc.sbuf_tensor(name, shape, dt))
        xg = sb("xg", [128, KC, Tg], F32)
        xh = sb("xh", [128, KC, Tg], BF16)
        xsq = sb("xsq", [128, KC, Tg], BF16)
        wx_sb = sb("wx_sb", [128, KC, 96], BF16)
        xp_sb = sb("xp_sb", [96, Tg], F32)
        ones_s = sb("ones_s", [1, 128], BF16)
        onescol = sb("onescol", [128, 1], BF16)
        cc = sb("cc", [1, 2], F32)
        ps = [st.enter_context(nc.psum_tensor(f"ps{i}", [128, 512], F32)) for i in range(3)]
        esem = {e: st.enter_context(nc.semaphore(f"es_{e}")) for e in ENGS}
        dpools = {e: [st.enter_context(nc.semaphore(f"ds_{e}{i}")) for i in range(NLANES)]
                  for e in ("sync", "gpsimd")}
        block = st.enter_context(nc.Block())
        p = Prog(nc)

        cst = [p.dma("sync", lambda e: e.dma_start(ones_s[:, :], onesrow[:, :])),
               p.dma("sync", lambda e: e.dma_start(onescol[:, :], bass.AP(onesrow, 0, [[0, 128], [1, 1]]))),
               p.dma("sync", lambda e: e.dma_start(cc[:, :], ccol[:, :])),
               p.dma("gpsimd", lambda e: e.dma_start(
                   wx_sb[:, :, :], bass.AP(w_xT, 0, [[96, 128], [128 * 96, KC], [1, 96]])))]
        p.op("sync", lambda e: e.nop(), waits=cst)
        LN = LNKit(nc, st, "ln", ps[0], ps[1], ones_s, onescol, cc, Tg)
        prev = []
        for g in range(ngroups):
            tg = slice(g * Tg, (g + 1) * Tg)
            dx = p.dma("sync", lambda e, tg=tg: e.dma_start(
                xg[:, :, :], bass.AP(x_fm, tg.start, [[TC, 128], [128 * TC, KC], [1, Tg]])),
                waits=list(prev))
            LN.emit(p, xg, xh, xsq, KC, [dx])
            def mmxp(e):
                for k in range(KC):
                    inst = e.matmul(ps[2][0:96, 0:Tg], wx_sb[:, k, :], xh[:, k, :],
                                    start=(k == 0), stop=(k == KC - 1))
                return inst
            p.op("tensor", mmxp)
            r = p.op("vector", lambda e: e.tensor_copy(xp_sb[:, :], ps[2][0:96, 0:Tg]))
            d1 = p.dma("sync", lambda e, tg=tg: e.dma_start(
                bass.AP(xp_out, tg.start, [[TC, 96], [1, Tg]]), xp_sb[:, :]), waits=[r])
            d2 = p.dma("gpsimd", lambda e, tg=tg: e.dma_start(
                bass.AP(xh_out, tg.start, [[TC, 128], [128 * TC, KC], [1, Tg]]),
                xh.ap()), waits=[r])
            prev = [d1, d2]
        fin = [(dpools[q][l], c) for q, l, c in p.final_dma_waits()]
        p.op("sync", lambda e: e.nop(), waits=fin)
        p.emit(block, esem, dpools)
    return nc


def host_mamba_pre_inputs(inputs, x2):
    maps = []
    for c in range(8):
        b, q = divmod(c, 4)
        tsl = slice(q * TC, (q + 1) * TC)
        maps.append({
            "x_fm": np.ascontiguousarray(x2[b, tsl].T.astype(np.float32)),
            "w_xT": np.ascontiguousarray(inputs["L1_attn_W_x"].T),
            "onesrow": np.ones((1, 128), BF),
            "ccol": np.array([[1e-5, -0.5]], np.float32),
        })
    return maps



TT_ALL = 16384
TBLK = 2048
NBLK = TT_ALL // TBLK


def build_mamba_scan(nblocks_lim=None):
    nc = bass.Bass("TRN2", target_bir_lowering=False)
    nblk = NBLK if not nblocks_lim else nblocks_lim

    xh_s = nc.dram_tensor("xh_s", [256, TT_ALL], BF16, kind="ExternalInput")
    dp_in = nc.dram_tensor("dp_in", [64, TT_ALL], F32, kind="ExternalInput")
    brows = nc.dram_tensor("brows", [16, TT_ALL], F32, kind="ExternalInput")
    crows = nc.dram_tensor("crows", [16, TT_ALL], F32, kind="ExternalInput")
    wdtT_s = nc.dram_tensor("wdtT_s", [64, 256], F32, kind="ExternalInput")
    bdt_c = nc.dram_tensor("bdt_c", [128, 2], F32, kind="ExternalInput")
    alog_s = nc.dram_tensor("alog_s", [128, 2 * 16], F32, kind="ExternalInput")
    dvec_c = nc.dram_tensor("dvec_c", [128, 2], F32, kind="ExternalInput")
    onesrow = nc.dram_tensor("onesrow", [1, 128], BF16, kind="ExternalInput")
    spc = nc.dram_tensor("spc", [128, 2], F32, kind="ExternalInput")
    y2_out = nc.dram_tensor("y2_out", [256, TT_ALL], BF16, kind="ExternalOutput")

    st = contextlib.ExitStack()
    with st:
        sb = lambda name, shape, dt=BF16: st.enter_context(nc.sbuf_tensor(name, shape, dt))
        dp16 = sb("dp16", [64, TT_ALL], BF16)
        wdt_sb = sb("wdt_sb", [64, 2, 128], BF16)
        bdt_sb = sb("bdt_sb", [128, 2], F32)
        acol = sb("acol", [128, 2, 16], F32)
        dvec_sb = sb("dvec_sb", [128, 2], F32)
        ones_s = sb("ones_s", [1, 128], BF16)
        spc_sb = sb("spc_sb", [128, 2], F32)
        xh_blk = sb("xh_blk", [128, 2, TBLK], BF16)
        du = sb("du", [128, TBLK], F32)
        t1 = sb("t1", [128, TBLK], F32)
        delta32 = sb("delta32", [128, 2, TBLK], F32)
        u16 = sb("u16", [128, 2, TBLK], BF16)
        abar = sb("abar", [128, TBLK], BF16)
        bx = sb("bx", [128, TBLK], BF16)
        h16 = sb("h16", [128, TBLK], BF16)
        hc = sb("hc", [128, TBLK], BF16)
        browf = sb("browf", [1, TBLK], F32)
        crowf = sb("crowf", [1, TBLK], F32)
        brow16 = sb("brow16", [1, TBLK], BF16)
        crow16 = sb("crow16", [1, TBLK], BF16)
        bb16 = sb("bb16", [128, TBLK], BF16)
        cb16 = sb("cb16", [128, TBLK], BF16)
        yacc = sb("yacc", [128, 2, TBLK], F32)
        carry = sb("carry", [128, 2, 16], F32)
        y2b = sb("y2b", [128, 2, TBLK], BF16)

        pd = st.enter_context(nc.psum_tensor("pd", [128, TBLK], F32))
        pb = st.enter_context(nc.psum_tensor("pb", [128, TBLK], F32))
        esem = {e: st.enter_context(nc.semaphore(f"es_{e}")) for e in ENGS}
        dpools = {e: [st.enter_context(nc.semaphore(f"ds_{e}{i}")) for i in range(NLANES)]
                  for e in ("sync", "gpsimd")}
        block = st.enter_context(nc.Block())
        p = Prog(nc)

        cst = [
            p.dma("gpsimd", lambda e: e.dma_start(dp16[:, :], dp_in[:, :])),
            p.dma("gpsimd", lambda e: e.dma_start(
                wdt_sb.ap().rearrange("p h d -> p (h d)"), wdtT_s[:, :])),
            p.dma("sync", lambda e: e.dma_start(bdt_sb[:, :], bdt_c[:, :])),
            p.dma("sync", lambda e: e.dma_start(
                acol.ap().rearrange("p h n -> p (h n)"), alog_s[:, :])),
            p.dma("sync", lambda e: e.dma_start(dvec_sb[:, :], dvec_c[:, :])),
            p.dma("sync", lambda e: e.dma_start(ones_s[:, :], onesrow[:, :])),
            p.dma("sync", lambda e: e.dma_start(spc_sb[:, :], spc[:, :])),
        ]
        p.op("scalar", lambda e: e.activation(
            acol.ap().rearrange("p h n -> p (h n)"),
            acol.ap().rearrange("p h n -> p (h n)"), AF.Exp), waits=cst)
        p.op("vector", lambda e: e.tensor_scalar(
            acol.ap().rearrange("p h n -> p (h n)"),
            acol.ap().rearrange("p h n -> p (h n)"), -1.0, None, op0=ALU.mult))
        p.op("vector", lambda e: e.memset(carry.ap().rearrange("p h n -> p (h n)"), 0.0))

        prev = []
        row_cp = []
        for B in range(nblk):
            bsl = slice(B * TBLK, (B + 1) * TBLK)
            dxh = p.dma("gpsimd", lambda e, bsl=bsl: e.dma_start(
                xh_blk[:, :, :], bass.AP(xh_s, bsl.start, [[TT_ALL, 128], [128 * TT_ALL, 2], [1, TBLK]])),
                waits=list(prev))
            for hf in range(2):
                def mmd(e, hf=hf, bsl=bsl):
                    for j in range(TBLK // 512):
                        inst = e.matmul(pd[:, j * 512:(j + 1) * 512], wdt_sb[:, hf, :],
                                        dp16[:, bsl.start + j * 512: bsl.start + (j + 1) * 512],
                                        start=True, stop=True)
                    return inst
                p.op("tensor", mmd)
                p.op("vector", lambda e, hf=hf: e.tensor_scalar(
                    du[:, :], pd[:, :], bdt_sb[:, hf:hf + 1], None, op0=ALU.add))
                p.op("scalar", lambda e: e.activation(t1[:, :], du[:, :], AF.Abs))
                p.op("scalar", lambda e: e.activation(t1[:, :], t1[:, :], AF.Exp, scale=spc_sb[:, 1:2]))
                p.op("scalar", lambda e: e.activation(t1[:, :], t1[:, :], AF.Ln, bias=spc_sb[:, 0:1]))
                p.op("vector", lambda e: e.tensor_scalar(du[:, :], du[:, :], 0.0, None, op0=ALU.max))
                p.op("vector", lambda e, hf=hf: e.tensor_tensor(delta32[:, hf, :], du[:, :], t1[:, :], op=ALU.add))
                p.op("vector", lambda e, hf=hf: e.tensor_tensor(
                    u16[:, hf, :], delta32[:, hf, :], xh_blk[:, hf, :], op=ALU.mult), waits=[dxh])
                p.op("vector", lambda e, hf=hf: e.memset(yacc[:, hf, :], 0.0))
            for n in range(16):
                dbr = p.dma("sync", lambda e, n=n, bsl=bsl: e.dma_start(
                    browf[:, :], bass.AP(brows, n * TT_ALL + bsl.start, [[1, 1], [1, TBLK]])),
                    waits=list(row_cp))
                dcr = p.dma("sync", lambda e, n=n, bsl=bsl: e.dma_start(
                    crowf[:, :], bass.AP(crows, n * TT_ALL + bsl.start, [[1, 1], [1, TBLK]])),
                    waits=list(row_cp))
                row_cp = []
                row_cp.append(p.op("vector", lambda e: e.tensor_copy(brow16[:, :], browf[:, :]), waits=[dbr]))
                row_cp.append(p.op("vector", lambda e: e.tensor_copy(crow16[:, :], crowf[:, :]), waits=[dcr]))
                def mmb(e):
                    for j in range(TBLK // 512):
                        inst = e.matmul(pb[:, j * 512:(j + 1) * 512], ones_s[:, :],
                                        brow16[0:1, j * 512:(j + 1) * 512], start=True, stop=True)
                    return inst
                p.op("tensor", mmb)
                p.op("vector", lambda e: e.tensor_copy(bb16[:, :], pb[:, :]))
                def mmc(e):
                    for j in range(TBLK // 512):
                        inst = e.matmul(pb[:, j * 512:(j + 1) * 512], ones_s[:, :],
                                        crow16[0:1, j * 512:(j + 1) * 512], start=True, stop=True)
                    return inst
                p.op("tensor", mmc)
                p.op("vector", lambda e: e.tensor_copy(cb16[:, :], pb[:, :]))
                for hf in range(2):
                    p.op("scalar", lambda e, hf=hf, n=n: e.activation(
                        abar[:, :], delta32[:, hf, :], AF.Exp, scale=acol[:, hf, n:n + 1]))
                    p.op("vector", lambda e, hf=hf: e.tensor_tensor(
                        bx[:, :], u16[:, hf, :], bb16[:, :], op=ALU.mult))
                    if B % 4 == 0:
                        p.op("vector", lambda e: e.tensor_tensor_scan(
                            h16[:, :], abar[:, :], bx[:, :], 0.0, op0=ALU.mult, op1=ALU.add))
                    else:
                        p.op("vector", lambda e, hf=hf, n=n: e.tensor_tensor_scan(
                            h16[:, :], abar[:, :], bx[:, :], carry[:, hf, n:n + 1],
                            op0=ALU.mult, op1=ALU.add))
                    p.op("vector", lambda e, hf=hf, n=n: e.tensor_copy(
                        carry[:, hf, n:n + 1], h16[:, TBLK - 1:TBLK]))
                    p.op("vector", lambda e: e.tensor_tensor(hc[:, :], h16[:, :], cb16[:, :], op=ALU.mult))
                    p.op("vector", lambda e, hf=hf: e.tensor_tensor(
                        yacc[:, hf, :], yacc[:, hf, :], hc[:, :], op=ALU.add))
            prev = []
            for hf in range(2):
                r = p.op("vector", lambda e, hf=hf: e.scalar_tensor_tensor(
                    y2b[:, hf, :], xh_blk[:, hf, :], dvec_sb[:, hf:hf + 1], yacc[:, hf, :],
                    op0=ALU.mult, op1=ALU.add))
                prev.append(p.dma("sync", lambda e, hf=hf, bsl=bsl: e.dma_start(
                    bass.AP(y2_out, hf * 128 * TT_ALL + bsl.start, [[TT_ALL, 128], [1, TBLK]]),
                    y2b[:, hf, :]), waits=[r]))
        fin = [(dpools[q][l], c) for q, l, c in p.final_dma_waits()]
        p.op("sync", lambda e: e.nop(), waits=fin)
        p.emit(block, esem, dpools)
    return nc


def host_mamba_scan_inputs(inputs, xh_full, xp_full):
    maps = []
    for c in range(8):
        ds = slice(256 * c, 256 * (c + 1))
        maps.append({
            "xh_s": np.ascontiguousarray(xh_full[ds]),
            "dp_in": np.ascontiguousarray(xp_full[0:64].astype(np.float32)),
            "brows": np.ascontiguousarray(xp_full[64:80].astype(np.float32)),
            "crows": np.ascontiguousarray(xp_full[80:96].astype(np.float32)),
            "wdtT_s": np.ascontiguousarray(inputs["L1_attn_W_dt"].T[:, ds]),
            "bdt_c": np.ascontiguousarray(inputs["L1_attn_b_dt"][ds].reshape(2, 128).T),
            "alog_s": np.ascontiguousarray(
                inputs["L1_attn_A_log"][ds].reshape(2, 128, 16).transpose(1, 0, 2).reshape(128, 32)),
            "dvec_c": np.ascontiguousarray(inputs["L1_attn_D"][ds].reshape(2, 128).T),
            "onesrow": np.ones((1, 128), BF),
            "spc": np.tile(np.array([[1.0, -1.0]], np.float32), (128, 1)),
        })
    return maps


NE = 8
IEX = 1024


def build_moe(Tt=512, ntiles_lim=None):
    nc = bass.Bass("TRN2", target_bir_lowering=False)
    ntiles = TC // Tt if not ntiles_lim else ntiles_lim
    KC = NCHUNK
    NSUB = Tt // 128

    x_fm = nc.dram_tensor("x_fm", [D, TC], F32, kind="ExternalInput")
    y2_fm = nc.dram_tensor("y2_fm", [D, TC], BF16, kind="ExternalInput")
    w1T = nc.dram_tensor("w1T", [NE * D, IEX], F32, kind="ExternalInput")
    w3T = nc.dram_tensor("w3T", [NE * D, IEX], F32, kind="ExternalInput")
    w2T = nc.dram_tensor("w2T", [NE * IEX, D], F32, kind="ExternalInput")
    ws1T = nc.dram_tensor("ws1T", [D, IEX], F32, kind="ExternalInput")
    ws3T = nc.dram_tensor("ws3T", [D, IEX], F32, kind="ExternalInput")
    ws2T = nc.dram_tensor("ws2T", [IEX, D], F32, kind="ExternalInput")
    gwT = nc.dram_tensor("gwT", [D, NE], F32, kind="ExternalInput")
    gb_r = nc.dram_tensor("gb_r", [1, NE], F32, kind="ExternalInput")
    onesrow = nc.dram_tensor("onesrow", [1, 128], BF16, kind="ExternalInput")
    ccol = nc.dram_tensor("ccol", [1, 2], F32, kind="ExternalInput")
    identin = nc.dram_tensor("identin", [128, 128], BF16, kind="ExternalInput")
    dws = nc.dram_tensor("dws", [NE, Tt], BF16, kind="Internal")
    out_fm = nc.dram_tensor("out_fm", [D, TC], F32, kind="ExternalOutput")

    st = contextlib.ExitStack()
    with st:
        sb = lambda name, shape, dt=BF16: st.enter_context(nc.sbuf_tensor(name, shape, dt))
        x3 = sb("x3", [128, KC, Tt], F32)
        y2g = sb("y2g", [128, KC, Tt], BF16)
        xh = sb("xh", [128, KC, Tt], BF16)
        he_all = sb("he_all", [128, NE + 1, 8, Tt], BF16)
        w1s = sb("w1s", [128, KC, 512], BF16)
        w3s = sb("w3s", [128, KC, 512], BF16)
        w2s = sb("w2s", [128, 8, 128], BF16)
        w2s2 = sb("w2s2", [128, 8, 128], BF16)
        gw_sb = sb("gw_sb", [128, KC, NE], BF16)
        gbb = sb("gbb", [128, NE], F32)
        lg = sb("lg", [128, NE], F32)
        nm = sb("nm", [128, 1], F32)
        sm = sb("sm", [128, 1], F32)
        sc32 = sb("sc32", [128, NE], F32)
        bias2 = sb("bias2", [128, NE], F32)
        eq1 = sb("eq1", [128, NE], F32)
        eq2 = sb("eq2", [128, NE], F32)
        dw32 = sb("dw32", [128, NE], F32)
        dw16 = sb("dw16", [128, NE], BF16)
        dwT16 = sb("dwT16", [NE, Tt], BF16)
        dwb = sb("dwb", [128, NE, Tt], BF16)
        s16 = sb("s16", [128, Tt], BF16)
        d16t = sb("d16t", [128, Tt], BF16)
        outsb = sb("outsb", [128, Tt], F32)
        ones_s = sb("ones_s", [1, 128], BF16)
        onescol = sb("onescol", [128, 1], BF16)
        id_sb = sb("id_sb", [128, 128], BF16)
        cc = sb("cc", [1, 2], F32)

        ps = [st.enter_context(nc.psum_tensor(f"ps{i}", [128, 512], F32)) for i in range(5)]
        psb = st.enter_context(nc.psum_tensor("psb", [128, 512], BF16))
        esem = {e: st.enter_context(nc.semaphore(f"es_{e}")) for e in ENGS}
        dpools = {e: [st.enter_context(nc.semaphore(f"ds_{e}{i}")) for i in range(NLANES)]
                  for e in ("sync", "gpsimd")}
        block = st.enter_context(nc.Block())
        p = Prog(nc)

        cst = [
            p.dma("sync", lambda e: e.dma_start(ones_s[:, :], onesrow[:, :])),
            p.dma("sync", lambda e: e.dma_start(onescol[:, :], bass.AP(onesrow, 0, [[0, 128], [1, 1]]))),
            p.dma("sync", lambda e: e.dma_start(cc[:, :], ccol[:, :])),
            p.dma("sync", lambda e: e.dma_start(id_sb[:, :], identin[:, :])),
            p.dma("sync", lambda e: e.dma_start(gbb[:, :], bass.AP(gb_r, 0, [[0, 128], [1, NE]]))),
            p.dma("gpsimd", lambda e: e.dma_start(
                gw_sb[:, :, :], bass.AP(gwT, 0, [[NE, 128], [128 * NE, KC], [1, NE]]))),
        ]
        p.op("sync", lambda e: e.nop(), waits=cst)
        LN = LNKit(nc, st, "ln", ps[0], ps[1], ones_s, onescol, cc, Tt)

        slot_last = {}
        prev = []
        for t in range(ntiles):
            tsl = slice(t * Tt, (t + 1) * Tt)
            dx = p.dma("sync", lambda e, tsl=tsl: e.dma_start(
                x3[:, :, :], bass.AP(x_fm, tsl.start, [[TC, 128], [128 * TC, KC], [1, Tt]])),
                waits=list(prev))
            dy = p.dma("gpsimd", lambda e, tsl=tsl: e.dma_start(
                y2g[:, :, :], bass.AP(y2_fm, tsl.start, [[TC, 128], [128 * TC, KC], [1, Tt]])),
                waits=list(prev))
            p.op("vector", lambda e: e.tensor_tensor(
                x3.ap().rearrange("p c t -> p (c t)"),
                x3.ap().rearrange("p c t -> p (c t)"),
                y2g.ap().rearrange("p c t -> p (c t)"), op=ALU.add), waits=[dx, dy])
            LN.emit(p, x3, xh, y2g, KC)

            for su in range(NSUB):
                ssl = slice(su * 128, (su + 1) * 128)
                def mmg(e, ssl=ssl):
                    for k in range(KC):
                        inst = e.matmul(ps[2][0:128, 0:NE], xh[:, k, ssl], gw_sb[:, k, :],
                                        start=(k == 0), stop=(k == KC - 1))
                    return inst
                p.op("tensor", mmg)
                p.op("vector", lambda e: e.tensor_copy(lg[:, :], ps[2][0:128, 0:NE]))
                p.op("vector", lambda e: e.reduce_max(nm[:, 0:1], lg[:, :], axis=AX.X, negate=True))
                p.op("scalar", lambda e: e.activation(sc32[:, :], lg[:, :], AF.Exp, bias=nm[:, 0:1]))
                p.op("vector", lambda e: e.reduce_sum(sm[:, 0:1], sc32[:, :], axis=AX.X))
                p.op("vector", lambda e: e.reciprocal(sm[:, 0:1], sm[:, 0:1]))
                p.op("vector", lambda e: e.tensor_scalar(sc32[:, :], sc32[:, :], sm[:, 0:1], None, op0=ALU.mult))
                p.op("vector", lambda e: e.tensor_tensor(bias2[:, :], sc32[:, :], gbb[:, :], op=ALU.add))
                p.op("vector", lambda e: e.reduce_max(nm[:, 0:1], bias2[:, :], axis=AX.X))
                p.op("vector", lambda e: e.tensor_scalar(eq1[:, :], bias2[:, :], nm[:, 0:1], None, op0=ALU.is_equal))
                p.op("vector", lambda e: e.scalar_tensor_tensor(
                    bias2[:, :], eq1[:, :], -1e30, bias2[:, :], op0=ALU.mult, op1=ALU.add))
                p.op("vector", lambda e: e.reduce_max(nm[:, 0:1], bias2[:, :], axis=AX.X))
                p.op("vector", lambda e: e.tensor_scalar(eq2[:, :], bias2[:, :], nm[:, 0:1], None, op0=ALU.is_equal))
                p.op("vector", lambda e: e.tensor_tensor(eq1[:, :], eq1[:, :], eq2[:, :], op=ALU.add))
                p.op("vector", lambda e: e.tensor_tensor(dw32[:, :], sc32[:, :], eq1[:, :], op=ALU.mult))
                p.op("vector", lambda e: e.tensor_copy(dw16[:, :], dw32[:, :]))
                p.op("tensor", lambda e: e.transpose(psb[0:NE, 0:128], dw16[:, :], id_sb[:, :]))
                p.op("vector", lambda e, ssl=ssl: e.tensor_copy(dwT16[:, ssl], psb[0:NE, 0:128]))
            rdw = p.op("vector", lambda e: e.tensor_copy(dwT16[0:1, 0:1], dwT16[0:1, 0:1]))
            dwar = [slot_last[id(dws)]] if id(dws) in slot_last else []
            ddw1 = p.dma("sync", lambda e: e.dma_start(dws[:, :], dwT16[:, :]), waits=[rdw] + dwar)
            dwrefs = []
            for e_i in range(NE):
                dwrefs.append(p.dma("sync", lambda e, e_i=e_i: e.dma_start(
                    dwb[:, e_i, :], bass.AP(dws, e_i * Tt, [[0, 128], [1, Tt]])), waits=[ddw1]))
            slot_last[id(dws)] = dwrefs[-1]

            for e_i in range(NE + 1):
                for half in range(2):
                    isl = slice(half * 512, (half + 1) * 512)
                    if e_i < NE:
                        src1 = bass.AP(w1T, e_i * D * IEX + half * 512, [[IEX, 128], [128 * IEX, KC], [1, 512]])
                        src3 = bass.AP(w3T, e_i * D * IEX + half * 512, [[IEX, 128], [128 * IEX, KC], [1, 512]])
                    else:
                        src1 = bass.AP(ws1T, half * 512, [[IEX, 128], [128 * IEX, KC], [1, 512]])
                        src3 = bass.AP(ws3T, half * 512, [[IEX, 128], [128 * IEX, KC], [1, 512]])
                    war1 = [slot_last[id(w1s)]] if id(w1s) in slot_last else []
                    war3 = [slot_last[id(w3s)]] if id(w3s) in slot_last else []
                    d1 = p.dma("gpsimd", lambda e, src1=src1: e.dma_start(w1s[:, :, :], src1), waits=war1)
                    d3 = p.dma("gpsimd", lambda e, src3=src3: e.dma_start(w3s[:, :, :], src3), waits=war3)
                    for sub in range(4):
                        ic = half * 4 + sub
                        def mm1(e, sub=sub):
                            for k in range(KC):
                                inst = e.matmul(ps[2][:, 0:Tt], w1s[:, k, sub * 128:(sub + 1) * 128],
                                                xh[:, k, :], start=(k == 0), stop=(k == KC - 1))
                            return inst
                        r1 = p.op("tensor", mm1, waits=[d1])
                        slot_last[id(w1s)] = r1
                        def mm3(e, sub=sub):
                            for k in range(KC):
                                inst = e.matmul(ps[3][:, 0:Tt], w3s[:, k, sub * 128:(sub + 1) * 128],
                                                xh[:, k, :], start=(k == 0), stop=(k == KC - 1))
                            return inst
                        r3 = p.op("tensor", mm3, waits=[d3])
                        slot_last[id(w3s)] = r3
                        p.op("scalar", lambda e: e.activation(s16[:, :], ps[2][:, 0:Tt], AF.Silu))
                        if e_i < NE:
                            p.op("vector", lambda e, e_i=e_i: e.tensor_tensor(
                                d16t[:, :], ps[3][:, 0:Tt], dwb[:, e_i, :], op=ALU.mult),
                                waits=[dwrefs[e_i]] if sub == 0 and half == 0 else [])
                            p.op("vector", lambda e, e_i=e_i, ic=ic: e.tensor_tensor(
                                he_all[:, e_i, ic, :], s16[:, :], d16t[:, :], op=ALU.mult))
                        else:
                            p.op("vector", lambda e, e_i=e_i, ic=ic: e.tensor_tensor(
                                he_all[:, e_i, ic, :], s16[:, :], ps[3][:, 0:Tt], op=ALU.mult))

            prev = []
            for dc in range(KC):
                slot = w2s if dc % 2 == 0 else w2s2
                war = [slot_last[id(slot)]] if id(slot) in slot_last else []
                for e_i in range(NE + 1):
                    if e_i < NE:
                        srcw2 = bass.AP(w2T, e_i * IEX * D + dc * 128, [[D, 128], [128 * D, 8], [1, 128]])
                    else:
                        srcw2 = bass.AP(ws2T, dc * 128, [[D, 128], [128 * D, 8], [1, 128]])
                    sslot = slot if e_i % 2 == 0 else (w2s2 if slot is w2s else w2s)
                    war = [slot_last[id(sslot)]] if id(sslot) in slot_last else []
                    dref = p.dma("gpsimd", lambda e, srcw2=srcw2, sslot=sslot: e.dma_start(
                        sslot[:, :, :], srcw2), waits=war)
                    def mmy(e, e_i=e_i, sslot=sslot):
                        for ic in range(8):
                            inst = e.matmul(ps[4][:, 0:Tt], sslot[:, ic, :], he_all[:, e_i, ic, :],
                                            start=(e_i == 0 and ic == 0), stop=(e_i == NE and ic == 7))
                        return inst
                    my = p.op("tensor", mmy, waits=[dref])
                    slot_last[id(sslot)] = my
                owar = [slot_last[id(outsb)]] if id(outsb) in slot_last else []
                r = p.op("vector", lambda e, dc=dc: e.tensor_tensor(
                    outsb[:, :], ps[4][:, 0:Tt], x3[:, dc, :], op=ALU.add), waits=owar)
                dso = p.dma("sync", lambda e, dc=dc, tsl=tsl: e.dma_start(
                    bass.AP(out_fm, dc * 128 * TC + tsl.start, [[TC, 128], [1, Tt]]),
                    outsb[:, :]), waits=[r])
                prev.append(dso)
                slot_last[id(outsb)] = dso
        fin = [(dpools[q][l], c) for q, l, c in p.final_dma_waits()]
        p.op("sync", lambda e: e.nop(), waits=fin)
        p.emit(block, esem, dpools)
    return nc


def host_moe_inputs(inputs, x2, y2):
    w1 = inputs["L1_ffn_W1"]
    w2 = inputs["L1_ffn_W2"]
    w3 = inputs["L1_ffn_W3"]
    w1T = np.ascontiguousarray(w1.transpose(0, 2, 1).reshape(NE * D, IEX))
    w3T = np.ascontiguousarray(w3.transpose(0, 2, 1).reshape(NE * D, IEX))
    w2T = np.ascontiguousarray(w2.transpose(0, 2, 1).reshape(NE * IEX, D))
    maps = []
    for c in range(8):
        b, q = divmod(c, 4)
        tsl = slice(q * TC, (q + 1) * TC)
        maps.append({
            "x_fm": np.ascontiguousarray(x2[b, tsl].T.astype(np.float32)),
            "y2_fm": np.ascontiguousarray(y2[b, tsl].T.astype(BF)),
            "w1T": w1T, "w3T": w3T, "w2T": w2T,
            "ws1T": np.ascontiguousarray(inputs["L1_ffn_Ws1"].T),
            "ws3T": np.ascontiguousarray(inputs["L1_ffn_Ws3"].T),
            "ws2T": np.ascontiguousarray(inputs["L1_ffn_Ws2"].T),
            "gwT": np.ascontiguousarray(inputs["L1_ffn_gate_w"].T),
            "gb_r": np.ascontiguousarray(inputs["L1_ffn_gate_b"].reshape(1, NE)),
            "onesrow": np.ones((1, 128), BF),
            "ccol": np.array([[1e-5, -0.5]], np.float32),
            "identin": np.eye(128, dtype=np.float32).astype(BF),
        })
    return maps


def kernel(**inputs):
    cores = list(range(8))

    nc1 = build_l0attn()
    res1 = run_bass_kernel_spmd(nc1, host_l0attn_inputs(inputs), core_ids=cores)
    y0 = np.zeros((2, 8192, 2048), np.float32)
    for c in cores:
        b = c // 4
        hs = slice(4 * (c % 4) * 128, (4 * (c % 4) + 4) * 128)
        y0[b, :, hs] = np.asarray(res1.results[c]["y_fm"]).astype(np.float32).T

    x = inputs["L0_attn_x"].astype(np.float32)
    nc2 = build_l0ffn()
    res2 = run_bass_kernel_spmd(nc2, host_l0ffn_inputs(inputs, x, y0), core_ids=cores)
    x2 = np.zeros((2, 8192, 2048), np.float32)
    for c in cores:
        b, q = divmod(c, 4)
        x2[b, q * 2048:(q + 1) * 2048] = np.asarray(res2.results[c]["x2_fm"]).T

    nc3a = build_mamba_pre()
    res3a = run_bass_kernel_spmd(nc3a, host_mamba_pre_inputs(inputs, x2), core_ids=cores)
    xh_full = np.zeros((2048, 16384), BF)
    xp_full = np.zeros((96, 16384), np.float32)
    for c in cores:
        b, q = divmod(c, 4)
        tsl = slice(b * 8192 + q * 2048, b * 8192 + (q + 1) * 2048)
        xh_full[:, tsl] = np.asarray(res3a.results[c]["xh_out"]).view(BF) \
            if np.asarray(res3a.results[c]["xh_out"]).dtype != BF else np.asarray(res3a.results[c]["xh_out"])
        xp_full[:, tsl] = np.asarray(res3a.results[c]["xp_out"])

    nc3b = build_mamba_scan()
    res3b = run_bass_kernel_spmd(nc3b, host_mamba_scan_inputs(inputs, xh_full, xp_full), core_ids=cores)
    y2 = np.zeros((2, 8192, 2048), np.float32)
    for c in cores:
        ds = slice(256 * c, 256 * (c + 1))
        y2[:, :, ds] = np.asarray(res3b.results[c]["y2_out"]).astype(np.float32).T.reshape(2, 8192, 256)

    nc4 = build_moe()
    res4 = run_bass_kernel_spmd(nc4, host_moe_inputs(inputs, x2, y2), core_ids=cores)
    out = np.zeros((2, 8192, 2048), np.float32)
    for c in cores:
        b, q = divmod(c, 4)
        out[b, q * 2048:(q + 1) * 2048] = np.asarray(res4.results[c]["out_fm"]).T
    return out
